# version 38
# baseline (speedup 1.0000x reference)
import numpy as np
from contextlib import ExitStack

import concourse.bass as bass
import concourse.mybir as mybir
from concourse.bass_utils import run_bass_kernel_spmd

F32 = mybir.dt.float32
BF16 = mybir.dt.bfloat16
AF = mybir.ActivationFunctionType
ALU = mybir.AluOpType

S = 4096
D = 1024
CC = 8
NT = 32
NP = 8
DHID = 4096
HCC = 32


def build_program():
    nc = bass.Bass("TRN2", target_bir_lowering=False)

    xT = nc.dram_tensor("xT", [D, S], F32, kind="ExternalInput").ap()
    keysT = nc.dram_tensor("keysT", [D, S], F32, kind="ExternalInput").ap()
    valuesT = nc.dram_tensor("valuesT", [D, S], F32, kind="ExternalInput").ap()
    wq_t = nc.dram_tensor("wq_t", [8, 128, 1024], F32, kind="ExternalInput").ap()
    wk_t = nc.dram_tensor("wk_t", [8, 128, 1024], F32, kind="ExternalInput").ap()
    wv_io = nc.dram_tensor("wv_io", [D, D], F32, kind="ExternalInput").ap()
    bq_t = nc.dram_tensor("bq_t", [128, 8], F32, kind="ExternalInput").ap()
    bk_t = nc.dram_tensor("bk_t", [128, 8], F32, kind="ExternalInput").ap()
    bv_row = nc.dram_tensor("bv_row", [1, D], F32, kind="ExternalInput").ap()
    w1_t = nc.dram_tensor("w1_t", [32, 128, 1024], F32, kind="ExternalInput").ap()
    w2_t = nc.dram_tensor("w2_t", [32, 128, 1024], F32, kind="ExternalInput").ap()
    h1_t = nc.dram_tensor("h1_t", [32, 128, 4096], F32, kind="ExternalInput").ap()
    h2_t = nc.dram_tensor("h2_t", [32, 128, 4096], F32, kind="ExternalInput").ap()
    w3_t = nc.dram_tensor("w3_t", [8, 128, 4096], F32, kind="ExternalInput").ap()
    s1_t = nc.dram_tensor("s1_t", [128, 32], F32, kind="ExternalInput").ap()
    s2_t = nc.dram_tensor("s2_t", [128, 32], F32, kind="ExternalInput").ap()
    sh1_t = nc.dram_tensor("sh1_t", [128, 32], F32, kind="ExternalInput").ap()
    sh2_t = nc.dram_tensor("sh2_t", [128, 32], F32, kind="ExternalInput").ap()
    s3_t = nc.dram_tensor("s3_t", [128, 8], F32, kind="ExternalInput").ap()
    ident = nc.dram_tensor("ident", [128, 128], F32, kind="ExternalInput").ap()

    yT = nc.dram_tensor("yT", [D, S], F32, kind="ExternalOutput").ap()

    qT_s = nc.dram_tensor("qT_s", [8, 128, S], BF16, kind="Internal").ap()
    kT_s = nc.dram_tensor("kT_s", [8, 128, S], BF16, kind="Internal").ap()
    v_s = nc.dram_tensor("v_s", [NT, 128, D], BF16, kind="Internal").ap()
    x2_s = nc.dram_tensor("x2_s", [D, S], F32, kind="Internal").ap()
    ln2_s = nc.dram_tensor("ln2_s", [D, S], BF16, kind="Internal").ap()
    wbf_s = nc.dram_tensor("wbf_s", [104, 128, 4096], BF16, kind="Internal").ap()
    m_row_s = nc.dram_tensor("m_row_s", [1, S], F32, kind="Internal").ap()
    r_row_s = nc.dram_tensor("r_row_s", [1, S], F32, kind="Internal").ap()

    ops = {"sync": [], "tensor": [], "scalar": [], "vector": []}
    cnt = {"ld": 0, "st": 0, "pe": 0, "act": 0, "dve": 0,
           "srm": 0, "sq0": 0, "sq1": 0, "sv0": 0, "sv1": 0,
           "sln0": 0, "sln1": 0, "sln2": 0, "sln3": 0,
           "swc0": 0, "swc1": 0, "sy0": 0, "sy1": 0,
           "lx0": 0, "lx1": 0, "lx2": 0, "lx3": 0,
           "lw0": 0, "lw1": 0, "lqk": 0}
    hwm = {}
    sems = {}

    def emit(eng, waits, make, incs=()):
        rw = []
        for sname, v in waits:
            if v is None or v <= 0:
                continue
            key = (eng, sname)
            if hwm.get(key, 0) >= v:
                continue
            hwm[key] = v
            rw.append((sname, v))
        for sname, n in incs:
            cnt[sname] += n
        mine = tuple(incs)

        def run(e, rw=tuple(rw), mk=make, inc=mine):
            for sname, v in rw:
                e.wait_ge(sems[sname], v)
            r = mk(e)
            for sname, n in inc:
                r = r.then_inc(sems[sname], n)
        ops[eng].append(run)

    def dma(out, in_, waits=(), kind="ld", q="sync"):
        emit(q, waits, (lambda e, o=out, i=in_: e.dma_start(out=o, in_=i)),
             [(kind, 16)])
        return cnt[kind]

    def mm(out, lhsT, rhs, start, stop, waits=(), inc=False):
        emit("tensor", waits,
             (lambda e, o=out, l=lhsT, r=rhs, a=start, b=stop:
              e.matmul(o, l, r, start=a, stop=b)),
             [("pe", 1)] if inc else ())
        return cnt["pe"]

    def tr(out, in_, idn, waits=(), inc=True):
        emit("tensor", waits,
             (lambda e, o=out, i=in_, d=idn: e.transpose(o, i, d)),
             [("pe", 1)] if inc else ())
        return cnt["pe"]

    def act(out, in_, func, waits=(), bias=None, scale=None, inc=True):
        kw = {}
        if bias is not None:
            kw["bias"] = bias
        if scale is not None:
            kw["scale"] = scale
        emit("scalar", waits,
             (lambda e, o=out, i=in_, f=func, k=kw: e.activation(o, i, f, **k)),
             [("act", 1)] if inc else ())
        return cnt["act"]

    def dve(name, *args, waits=(), inc=True):
        emit("vector", waits,
             (lambda e, n=name, a=args: getattr(e, n)(*a)),
             [("dve", 1)] if inc else ())
        return cnt["dve"]

    stack = ExitStack()

    def sb(name, shape, dtype):
        return stack.enter_context(nc.sbuf_tensor(name, shape, dtype))

    ident32 = sb("ident32", [128, 128], F32)
    onesrow = sb("onesrow", [1, 128], F32)
    onescol = sb("onescol", [128, 1], BF16)
    eps1 = sb("eps1", [1, 1], F32)
    s1_sb = sb("s1_sb", [128, 32], F32)
    s2_sb = sb("s2_sb", [128, 32], F32)
    sh1_sb = sb("sh1_sb", [128, 32], F32)
    sh2_sb = sb("sh2_sb", [128, 32], F32)
    s3_sb = sb("s3_sb", [128, 8], F32)

    strip1 = sb("strip1", [128, CC * S], BF16)
    strip2 = sb("strip2", [128, CC * S], BF16)
    m_tmp2 = sb("m_tmp2", [1, 512], F32)
    r_tmp2 = sb("r_tmp2", [1, 512], F32)

    dma(ident32[:], ident)
    dma(s1_sb[:], s1_t)
    dma(s2_sb[:], s2_t)
    dma(sh1_sb[:], sh1_t)
    dma(sh2_sb[:], sh2_t)
    dma(s3_sb[:], s3_t)
    dve("memset", onesrow[:], 1.0)
    dve("memset", onescol[:], 1.0 / 1024.0, inc=False)
    dve("memset", eps1[:], 1e-5, inc=False)

    esA = ExitStack()

    def sbA(name, shape, dtype):
        return esA.enter_context(nc.sbuf_tensor(name, shape, dtype))

    bq_sb = sbA("bq_sb", [128, 8], F32)
    bk_sb = sbA("bk_sb", [128, 8], F32)
    bv_sb = sbA("bv_sb", [1, D], F32)
    dma(bq_sb[:], bq_t)
    dma(bk_sb[:], bk_t)
    dma(bv_sb[:], bv_row)
    ld_setup = cnt["ld"]

    xf = [sbA("xf%d" % i, [128, 1024], F32) for i in range(4)]
    xf_free = [[] for _ in range(4)]
    for cc in range(CC):
        for q4 in range(4):
            i = q4 % 4
            lxn = "lx%d" % i
            dma(xf[i][:], xT[cc * 128:(cc + 1) * 128, q4 * 1024:(q4 + 1) * 1024],
                waits=xf_free[i], kind=lxn)
            off = cc * S + q4 * 1024
            emit("scalar", [(lxn, cnt[lxn])],
                 (lambda e, o=strip1[:, off:off + 1024], s=xf[i][:]:
                  e.copy(o, s)), [("act", 1)])
            csq = dve("tensor_tensor", strip2[:, off:off + 1024], xf[i][:],
                      xf[i][:], ALU.mult, waits=[(lxn, cnt[lxn])])
            xf_free[i] = [("act", cnt["act"]), ("dve", csq)]
    cast_done = ("act", cnt["act"])
    cast_sq_done = ("dve", cnt["dve"])

    pm = [esA.enter_context(nc.psum_tensor("pm0", [1, 512], F32)),
          esA.enter_context(nc.psum_tensor("pm1", [1, 512], F32))]
    pq = [esA.enter_context(nc.psum_tensor("pq0", [1, 512], F32)),
          esA.enter_context(nc.psum_tensor("pq1", [1, 512], F32))]
    pbb = esA.enter_context(nc.psum_tensor("pbb", [128, 512], F32))
    m_tmp = sbA("m_tmp", [1, 512], F32)
    r_tmp = sbA("r_tmp", [1, 512], F32)
    varrow = sbA("varrow", [1, 512], F32)
    msq = sbA("msq", [1, 512], F32)
    mBst = sbA("mBst", [128, 512], F32)
    rstdBst = sbA("rstdBst", [128, 512], F32)
    t1scr = sbA("t1scr", [128, 512], F32)

    stat_free = [0, 0]
    row_store = 0
    pbb_free = 0
    lnxb_done = 0
    lnxb_st = [0] * 8
    pe_stats = [0] * 8

    def a0_stats(st):
        par = st % 2
        for cc in range(CC):
            mm(pm[par][:], onescol[:],
               strip1[:, cc * S + st * 512: cc * S + st * 512 + 512],
               start=(cc == 0), stop=(cc == CC - 1),
               waits=[cast_done, ("dve", stat_free[par])])
        for cc in range(CC):
            mm(pq[par][:], onescol[:],
               strip2[:, cc * S + st * 512: cc * S + st * 512 + 512],
               start=(cc == 0), stop=(cc == CC - 1),
               waits=[cast_sq_done] if cc == 0 else (), inc=(cc == CC - 1))
        pe_stats[st] = cnt["pe"]

    def a0_rows_lnxb(st):
        nonlocal row_store, pbb_free, lnxb_done
        par = st % 2
        dve("tensor_copy", m_tmp[:], pm[par][:],
            waits=[("pe", pe_stats[st]), ("srm", row_store)], inc=False)
        dve("tensor_tensor", msq[:], m_tmp[:], m_tmp[:], ALU.mult, inc=False)
        stat_free[par] = dve("tensor_tensor", varrow[:], pq[par][:], msq[:],
                             ALU.subtract)
        m_evac = stat_free[par]
        act(varrow[:], varrow[:], AF.Ln, waits=[("dve", stat_free[par])],
            bias=eps1[:], inc=False)
        rstd_st = act(r_tmp[:], varrow[:], AF.Exp, scale=-0.5,
                      waits=[("srm", row_store)])
        mm(pbb[:], onesrow[:], m_tmp[:], True, True,
           waits=[("dve", max(pbb_free, m_evac))], inc=True)
        dve("tensor_copy", mBst[:], pbb[:], waits=[("pe", cnt["pe"])])
        mm(pbb[:], onesrow[:], r_tmp[:], True, True,
           waits=[("dve", cnt["dve"]), ("act", rstd_st)], inc=True)
        bcast_pe = cnt["pe"]
        pbb_free = dve("tensor_copy", rstdBst[:], pbb[:], waits=[("pe", bcast_pe)])
        dma(m_row_s[0:1, st * 512:(st + 1) * 512], m_tmp[:],
            waits=[("dve", m_evac), ("pe", bcast_pe)], kind="srm", q="scalar")
        dma(r_row_s[0:1, st * 512:(st + 1) * 512], r_tmp[:],
            waits=[("act", rstd_st)], kind="srm", q="scalar")
        row_store = cnt["srm"]
        for cc in range(CC):
            sl = strip1[:, cc * S + st * 512: cc * S + st * 512 + 512]
            dve("tensor_copy", t1scr[:], sl, inc=False)
            dve("tensor_tensor", t1scr[:], t1scr[:], mBst[:], ALU.subtract, inc=False)
            lnxb_done = dve("tensor_tensor", sl, t1scr[:], rstdBst[:], ALU.mult)
        lnxb_st[st] = lnxb_done

    for st in range(8):
        a0_stats(st)
        if st > 0:
            a0_rows_lnxb(st - 1)
    a0_rows_lnxb(7)

    wob = [sbA("wob0", [128, 1024], F32), sbA("wob1", [128, 1024], F32)]
    wpb = [sbA("wpb0", [128, 1024], BF16), sbA("wpb1", [128, 1024], BF16)]
    wvb = sbA("wvb", [128, 8 * 1024], BF16)
    qout = [sbA("qout0", [128, 512], BF16), sbA("qout1", [128, 512], BF16)]
    pproj = [esA.enter_context(nc.psum_tensor("pproj0", [128, 512], F32)),
             esA.enter_context(nc.psum_tensor("pproj1", [128, 512], F32))]

    wob_free = [0, 0]
    wpb_free = [0, 0]
    qout_free = [0, 0]
    pslot_free = [0, 0]
    qk_store = {"q": [(0, 0)] * 8, "k": [(0, 0)] * 8}
    q_pe_done = 0

    def proj_stage(nm, w_src, bias_sb, rhs_strip, rhs_ready, out_dram, store_list,
                   interleave=None):
        for oc in range(8):
            i = oc % 2
            lwn = "lw%d" % i
            dma(wob[i][:], w_src[oc], waits=[("dve", wob_free[i])], kind=lwn)
            wob_free[i] = dve("tensor_copy", wpb[i][:], wob[i][:],
                              waits=[(lwn, cnt[lwn]), ("pe", wpb_free[i])])
            if interleave is not None:
                interleave(oc)
            wcast = cnt["dve"]
            for st in range(8):
                slot = (oc * 8 + st) % 2
                rdy = rhs_ready[st] if isinstance(rhs_ready, list) else rhs_ready
                for cc in range(CC):
                    mm(pproj[slot][:],
                       wpb[i][:, cc * 128: cc * 128 + 128],
                       rhs_strip[:, cc * S + st * 512: cc * S + st * 512 + 512],
                       start=(cc == 0), stop=(cc == CC - 1),
                       waits=[("dve", max(rdy, wcast)),
                              ("act", pslot_free[slot])],
                       inc=(cc == CC - 1))
                pslot_free[slot] = act(
                    qout[st % 2][:], pproj[slot][:], AF.Relu,
                    waits=[("pe", cnt["pe"]), ("ld", ld_setup),
                           ("sq%d" % (st % 2), qout_free[st % 2])],
                    bias=bias_sb[:, oc:oc + 1])
                sqn = "sq%d" % (st % 2)
                dma(out_dram[oc, :, st * 512:(st + 1) * 512], qout[st % 2][:],
                    waits=[("act", cnt["act"])], kind=sqn, q="scalar")
                qout_free[st % 2] = cnt[sqn]
            wpb_free[i] = cnt["pe"]
            store_list[oc] = (cnt["sq0"], cnt["sq1"])

    keys_q = [0] * 4

    def emit_keysb(oc):
        for q4, cc in [divmod(4 * oc + k, CC) for k in range(4)]:
            i = cc % 4
            lxn = "lx%d" % i
            dma(xf[i][:], keysT[cc * 128:(cc + 1) * 128, q4 * 1024:(q4 + 1) * 1024],
                waits=xf_free[i], kind=lxn, q="scalar")
            off = cc * S + q4 * 1024
            xf_free[i] = [("dve", dve("tensor_copy", strip2[:, off:off + 1024],
                                      xf[i][:],
                                      waits=[(lxn, cnt[lxn]),
                                             ("pe", pe_stats[7])]))]
            if cc == CC - 1:
                keys_q[q4] = cnt["dve"]

    proj_stage("q", wq_t, bq_sb, strip1, lnxb_st, qT_s, qk_store["q"],
               interleave=emit_keysb)
    q_pe_done = cnt["pe"]
    keys_st = [keys_q[st // 2] for st in range(8)]

    vals_q = [0] * 4

    def emit_valsb(oc):
        for q4, cc in [divmod(4 * oc + k, CC) for k in range(4)]:
            i = cc % 4
            lxn = "lx%d" % i
            dma(xf[i][:], valuesT[cc * 128:(cc + 1) * 128, q4 * 1024:(q4 + 1) * 1024],
                waits=[*xf_free[i], ("pe", q_pe_done)], kind=lxn, q="scalar")
            off = cc * S + q4 * 1024
            xf_free[i] = [("dve", dve("tensor_copy", strip1[:, off:off + 1024],
                                      xf[i][:], waits=[(lxn, cnt[lxn])]))]
            if cc == CC - 1:
                vals_q[q4] = cnt["dve"]

    proj_stage("k", wk_t, bk_sb, strip2, keys_st, kT_s, qk_store["k"],
               interleave=emit_valsb)

    wv_cast = 0
    for ccw in range(8):
        i = ccw % 2
        lwn = "lw%d" % i
        dma(wob[i][:], wv_io[ccw * 128:(ccw + 1) * 128, :],
            waits=[("dve", wob_free[i])], kind=lwn)
        wob_free[i] = dve("tensor_copy", wvb[:, ccw * 1024:(ccw + 1) * 1024],
                          wob[i][:], waits=[(lwn, cnt[lwn]), ("pe", wpb_free[i])])
        wv_cast = cnt["dve"]

    bvB = sbA("bvB", [128, 1024], F32)
    mm(pbb[:], onesrow[:], bv_sb[0:1, 0:512], True, True,
       waits=[("dve", pbb_free), ("ld", ld_setup)], inc=True)
    dve("tensor_copy", bvB[:, 0:512], pbb[:], waits=[("pe", cnt["pe"])])
    mm(pbb[:], onesrow[:], bv_sb[0:1, 512:1024], True, True,
       waits=[("dve", cnt["dve"])], inc=True)
    pbb_free = dve("tensor_copy", bvB[:, 512:1024], pbb[:], waits=[("pe", cnt["pe"])])


    vout = [sbA("vout0", [128, 512], BF16), sbA("vout1", [128, 512], BF16)]
    vscr = sbA("vscr", [128, 512], F32)
    vout_free = [0, 0]
    vslot_free = [0, 0]
    v_store = (0, 0)
    for tc in range(NT):
        for ot in range(2):
            slot = (tc * 2 + ot) % 2
            for cc in range(CC):
                mm(pproj[slot][:],
                   strip1[:, cc * S + tc * 128: cc * S + tc * 128 + 128],
                   wvb[:, cc * 1024 + ot * 512: cc * 1024 + ot * 512 + 512],
                   start=(cc == 0), stop=(cc == CC - 1),
                   waits=[("dve", max(vals_q[tc // 8], wv_cast, pbb_free)),
                          ("dve", vslot_free[slot]),
                          ("act", pslot_free[slot])],
                   inc=(cc == CC - 1))
            dve("tensor_tensor", vscr[:], pproj[slot][:],
                bvB[:, ot * 512:(ot + 1) * 512], ALU.add,
                waits=[("pe", cnt["pe"]), ("sv%d" % slot, vout_free[slot])],
                inc=False)
            vslot_free[slot] = dve("tensor_scalar_max", vout[slot][:], vscr[:], 0.0)
            svn = "sv%d" % slot
            dma(v_s[tc, :, ot * 512:(ot + 1) * 512], vout[slot][:],
                waits=[("dve", cnt["dve"])], kind=svn, q="scalar")
            vout_free[slot] = cnt[svn]
            v_store = (cnt["sv0"], cnt["sv1"])
    a1_pe = cnt["pe"]
    a1_dve = cnt["dve"]

    esA.close()
    esB = ExitStack()

    def sbB(name, shape, dtype):
        return esB.enter_context(nc.sbuf_tensor(name, shape, dtype))

    qp = [strip1[:, 0:S], strip1[:, S:2 * S]]
    kz = [[strip1[:, 2 * S:3 * S], strip1[:, 3 * S:4 * S]],
          [strip1[:, 4 * S:5 * S], strip1[:, 5 * S:6 * S]]]
    P_all = strip2[:, 0:2 * NT * 512]

    xcol = [sbB("xcol0", [128, CC * 512], F32), sbB("xcol1", [128, CC * 512], F32)]
    va = [sbB("va0", [128, NT * 130], BF16), sbB("va1", [128, NT * 130], BF16)]
    mBp = sbB("mBp", [128, 512], F32)
    rstdBp = sbB("rstdBp", [128, 512], F32)
    saT = [sbB("saT0", [65, 512], F32), sbB("saT1", [65, 512], F32)]
    ctmp = sbB("ctmp", [128, 64], F32)
    srec = sbB("srec", [128, 1], F32)
    t2scr = sbB("t2scr", [128, 512], F32)
    scr2 = [strip1[:, 6 * S + i * 512: 6 * S + (i + 1) * 512] for i in range(CC)]
    ln2out = [strip1[:, 7 * S + i * 512: 7 * S + (i + 1) * 512] for i in range(4)]
    m2t = sbB("m2t", [1, 512], F32)
    v2t = sbB("v2t", [1, 512], F32)
    msq2t = sbB("msq2t", [1, 512], F32)
    mB2a = sbB("mB2a", [128, 512], F32)
    rB2a = sbB("rB2a", [128, 512], F32)

    pS = [esB.enter_context(nc.psum_tensor("pS0", [128, 1024], F32)),
          esB.enter_context(nc.psum_tensor("pS1", [128, 1024], F32))]
    pAV = [esB.enter_context(nc.psum_tensor("pAV0", [65, 512], F32)),
           esB.enter_context(nc.psum_tensor("pAV1", [65, 512], F32))]
    pTR = esB.enter_context(nc.psum_tensor("pTR", [128, 260], F32))
    pbb2 = esB.enter_context(nc.psum_tensor("pbb2", [128, 512], F32))

    va_ones = [0, 0]
    for i in range(2):
        va_ones[i] = dve(
            "memset",
            va[i][:].rearrange("p (g c) -> p g c", c=65)[:, :, 64:65], 1.0)
    kz_zero = 0
    for i in range(2):
        dve("memset", kz[i][0][64:128, :], 0.0, waits=[("pe", a1_pe)], inc=False)
        kz_zero = dve("memset", kz[i][1][0:64, :], 0.0)

    pair_qk_ld = [0] * (NP + 1)
    pair_xc_ld = [0] * (NP + 1)

    def prefetch_pair(p, pe_free, dve_free):
        buf = p % 2
        dma(qp[buf], qT_s[p], waits=[("pe", max(pe_free, a1_pe)),
                                     ("dve", max(a1_dve, kz_zero)),
                                     ("sq0", qk_store["q"][p][0]),
                                     ("sq1", qk_store["q"][p][1])], kind="lqk")
        dma(kz[buf][0][0:64, :], kT_s[p, 0:64, :],
            waits=[("sq0", qk_store["k"][p][0]),
                   ("sq1", qk_store["k"][p][1])], kind="lqk")
        dma(kz[buf][1][64:128, :], kT_s[p, 64:128, :], kind="lqk")
        pair_qk_ld[p] = cnt["lqk"]
        dma(xcol[buf][:].rearrange("p (cc s) -> p cc s", cc=CC),
            xT[:, 512 * p: 512 * p + 512].rearrange("(cc p) s -> p cc s", p=128),
            waits=[("dve", dve_free)])
        pair_xc_ld[p] = cnt["ld"]
        for h in (0, 1):
            dst = va[buf][:].rearrange("p (tc g) -> p tc g", g=130)[:, :, 65 * h: 65 * h + 64]
            srcv = v_s[:, :, 128 * p + 64 * h: 128 * p + 64 * h + 64] \
                .rearrange("tc p d -> p tc d")
            dma(dst, srcv, waits=[("sv0", v_store[0]), ("sv1", v_store[1]),
                                  ("dve", va_ones[buf])])
        return cnt["ld"]

    pair_pe_done = [0] * (NP + 2)
    pair_ld = [0] * (NP + 1)
    x2_store = [0] * (NP + 2)
    ln2_store = [(0, 0, 0, 0)] * (NP + 2)
    xcol_free = [0] * (NP + 2)

    exp_cnt = {}
    av_prog = {}
    sa_copy = {0: 0, 1: 0}
    tr_done = {0: 0, 1: 0}
    blk_done = {0: 0, 1: 0}
    pbb2_free = 0
    mB_used = 0
    bcast_pe2 = 0
    scr2_free = 0
    ln2out_free = [0, 0, 0, 0]
    v2_done = [0]

    def emit_av_chunks(g, tcs):
        pp, pj = g // 8, g % 8
        pbuf = pp % 2
        for h in (0, 1):
            for tc in tcs:
                mm(pAV[h][:],
                   va[pbuf][:, 130 * tc + 65 * h: 130 * tc + 65 * h + 65],
                   P_all[:, 1024 * tc + 512 * h: 1024 * tc + 512 * h + 512],
                   start=(tc == 0), stop=(tc == NT - 1),
                   waits=[("act", exp_cnt[(pj, tc)]),
                          ("dve", sa_copy[h]),
                          ("ld", max(pair_ld[pp], rows_ld_d[pp]))],
                   inc=True)
                av_prog[(h, pj, tc)] = cnt["pe"]

    def emit_epilogue_a(g):
        pp, pj = g // 8, g % 8
        for h in (0, 1):
            sa_copy[h] = dve("tensor_copy", saT[h][:], pAV[h][:],
                             waits=[("pe", av_prog[(h, pj, NT - 1)]),
                                    ("pe", tr_done[h])])

    def emit_epilogue_b(g, h):
        pp, pj = g // 8, g % 8
        pbuf = pp % 2
        for b in range(4):
            tr(pTR[:, 65 * b: 65 * b + 65], saT[h][0:65, 128 * b: 128 * b + 128],
               ident32[0:65, 0:65],
               waits=[("dve", sa_copy[h]),
                      ("dve", blk_done[0]), ("dve", blk_done[1])],
               inc=(b == 3))
        tr_done[h] = cnt["pe"]
        for b in range(4):
            dve("reciprocal", srec[:], pTR[:, 65 * b + 64: 65 * b + 65],
                waits=[("pe", tr_done[h])], inc=False)
            dve("tensor_copy", ctmp[:], pTR[:, 65 * b: 65 * b + 64], inc=False)
            dve("tensor_scalar_mul", ctmp[:], ctmp[:], srec[:], inc=False)
            col0 = 256 * h + (pj // 2)
            ccx = 4 * (pj % 2) + b
            slx = xcol[pbuf][:, ccx * 512 + col0: ccx * 512 + col0 + 253: 4]
            blk_done[h] = dve("tensor_tensor", slx, slx, ctmp[:], ALU.add)
        if pj == 7 and h == 1:
            pair_pe_done[pp] = cnt["pe"]

    def emit_epilogue(g):
        emit_epilogue_a(g)
        emit_epilogue_b(g, 0)
        emit_epilogue_b(g, 1)

    stats_mm_q = []

    def finish_a(pp):
        nonlocal scr2_free, pbb2_free
        pbuf = pp % 2
        dma(x2_s[:, 512 * pp: 512 * pp + 512].rearrange("(cc p) s -> p cc s", p=128),
            xcol[pbuf][:].rearrange("p (cc s) -> p cc s", cc=CC),
            waits=[("dve", blk_done[0]), ("dve", blk_done[1])], kind="st")
        x2_store[pp] = cnt["st"]
        for ccx in range(CC):
            sl = xcol[pbuf][:, ccx * 512:(ccx + 1) * 512]
            cd = dve("tensor_copy", scr2[ccx][:], sl, waits=[("pe", scr2_free)])
            stats_mm_q.append((0, ccx, cd, sl))

    def drain_stats_mms(nmax):
        nonlocal scr2_free, pbb2_free
        for _ in range(nmax):
            if not stats_mm_q:
                return
            sq, ccx, cd, sl = stats_mm_q.pop(0)
            mm(pbb2[0:1, :], onescol[:], scr2[ccx][:],
               start=(ccx == 0), stop=(ccx == CC - 1),
               waits=[("dve", cd), ("dve", pbb2_free)], inc=True)
            if sq == 0:
                cd2 = dve("tensor_tensor", scr2[ccx][:], sl, sl, ALU.mult,
                          waits=[("pe", cnt["pe"])])
                stats_mm_q.append((1, ccx, cd2, sl))
            if ccx == CC - 1:
                row = v2t if sq else m2t
                pbb2_free = dve("tensor_copy", row[:], pbb2[0:1, :],
                                waits=[("pe", cnt["pe"])])
                if sq:
                    scr2_free = cnt["pe"]
                    dve("tensor_tensor", msq2t[:], m2t[:], m2t[:], ALU.mult,
                        inc=False)
                    v2_done[0] = dve("tensor_tensor", v2t[:], v2t[:], msq2t[:],
                                     ALU.subtract)

    rstd_act = [0]

    def finish_b_act(pp):
        act(v2t[:], v2t[:], AF.Ln, waits=[("dve", v2_done[0])],
            bias=eps1[:], inc=False)
        rstd_act[0] = act(v2t[:], v2t[:], AF.Exp, scale=-0.5)

    def finish_b_mm1(pp):
        mm(pbb2[:], onesrow[:], m2t[:], True, True,
           waits=[("dve", pbb2_free)], inc=True)
        dve("tensor_copy", mB2a[:], pbb2[:], waits=[("pe", cnt["pe"])])

    def finish_b_mm2(pp):
        nonlocal pbb2_free
        mm(pbb2[:], onesrow[:], v2t[:], True, True,
           waits=[("dve", cnt["dve"]), ("act", rstd_act[0])], inc=True)
        pbb2_free = dve("tensor_copy", rB2a[:], pbb2[:], waits=[("pe", cnt["pe"])])

    def finish_b(pp, ccxs=range(CC)):
        pbuf = pp % 2
        for ccx in ccxs:
            sl = xcol[pbuf][:, ccx * 512:(ccx + 1) * 512]
            slnn = "sln%d" % (ccx % 4)
            dve("tensor_tensor", t2scr[:], sl, mB2a[:], ALU.subtract, inc=False)
            lnd = dve("tensor_tensor", ln2out[ccx % 4][:], t2scr[:], rB2a[:],
                      ALU.mult, waits=[(slnn, ln2out_free[ccx % 4])])
            dma(ln2_s[ccx * 128:(ccx + 1) * 128, 512 * pp: 512 * pp + 512],
                ln2out[ccx % 4][:], waits=[("dve", lnd)], kind=slnn, q="scalar")
            ln2out_free[ccx % 4] = cnt[slnn]
        if ccxs[-1] == CC - 1:
            ln2_store[pp] = (cnt["sln0"], cnt["sln1"], cnt["sln2"], cnt["sln3"])
            xcol_free[pp] = cnt["dve"]

    rows_ld_d = [0] * (NP + 1)

    def load_rows(pp):
        dma(m_tmp2[:], m_row_s[0:1, 512 * pp: 512 * pp + 512],
            waits=[("srm", row_store), ("pe", bcast_pe2)])
        dma(r_tmp2[:], r_row_s[0:1, 512 * pp: 512 * pp + 512])
        rows_ld_d[pp] = cnt["ld"]

    def pair_top(pp):
        nonlocal bcast_pe2, pbb2_free, mB_used
        pbuf = pp % 2
        if not rows_ld_d[pp]:
            load_rows(pp)
        mm(pbb2[:], onesrow[:], m_tmp2[:], True, True,
           waits=[("dve", max(pbb2_free, mB_used)), ("ld", rows_ld_d[pp])],
           inc=True)
        dve("tensor_copy", mBp[:], pbb2[:], waits=[("pe", cnt["pe"])])
        mm(pbb2[:], onesrow[:], r_tmp2[:], True, True,
           waits=[("dve", cnt["dve"])], inc=True)
        bcast_pe2 = cnt["pe"]
        pbb2_free = dve("tensor_copy", rstdBp[:], pbb2[:],
                        waits=[("pe", bcast_pe2)])
        for ccx in range(CC):
            sl = xcol[pbuf][:, ccx * 512:(ccx + 1) * 512]
            dve("tensor_tensor", t2scr[:], sl, mBp[:], ALU.subtract,
                waits=[("ld", max(pair_ld[pp], rows_ld_d[pp]))], inc=False)
            dve("tensor_tensor", t2scr[:], t2scr[:], rstdBp[:], ALU.mult,
                inc=False)
            mB_used = dve("tensor_tensor", sl, sl, t2scr[:], ALU.add)

    load_rows(0)
    pair_ld[0] = prefetch_pair(0, 0, 0)
    pair_top(0)
    for g in range(8 * NP):
        p, j = g // 8, g % 8
        buf = p % 2

        for s in range(32):
            gen = s % 2
            prev_exp = (exp_cnt.get((j, s - 2), 0) if s >= 2
                        else exp_cnt.get(((j - 1) % 8, 30 + s), 0))
            mm(pS[gen][:, 0:512],
               kz[buf][0][:, s * 128: s * 128 + 128],
               qp[buf][:, j * 512: j * 512 + 512],
               True, True,
               waits=[("act", prev_exp), ("lqk", pair_qk_ld[p])])
            pe_sc = mm(pS[gen][:, 512:1024],
                       kz[buf][1][:, s * 128: s * 128 + 128],
                       qp[buf][:, j * 512: j * 512 + 512],
                       True, True, inc=True)
            if g >= 1 and s < 16:
                emit_av_chunks(g - 1, range(2 * s, 2 * s + 2))
            if s == 16 and g >= 1:
                emit_epilogue_a(g - 1)
            if s == 17 and g >= 1:
                emit_epilogue_b(g - 1, 0)
            if s == 18 and g >= 1:
                emit_epilogue_b(g - 1, 1)
            if s == 19 and j == 1 and p >= 1:
                finish_a(p - 1)
            if j == 1 and s > 19:
                drain_stats_mms(3)
            if s == 16 and j == 2:
                drain_stats_mms(16)
                if p >= 1:
                    finish_b_act(p - 1)
            if s == 20 and j == 2 and p >= 1:
                finish_b_mm1(p - 1)
            if s == 26 and j == 2 and p >= 1:
                finish_b_mm2(p - 1)
            if s == 20 and j == 4 and p >= 1:
                finish_b(p - 1, range(0, 4))
            if s == 26 and j == 4 and p >= 1:
                finish_b(p - 1, range(4, CC))
            wav = max(av_prog.get((0, (j - 1) % 8, s), 0),
                      av_prog.get((1, (j - 1) % 8, s), 0))
            exp_cnt[(j, s)] = act(
                P_all[:, 1024 * s: 1024 * (s + 1)], pS[gen][:], AF.Exp,
                waits=[("pe", max(wav, pe_sc))],
                scale=0.125)
        if j == 4 and p + 1 < NP:
            pair_ld[p + 1] = prefetch_pair(
                p + 1, pair_pe_done[p - 1] if p else 0,
                xcol_free[p - 1] if p else 0)
        if j == 6 and p + 1 < NP:
            pair_top(p + 1)

    emit_av_chunks(8 * NP - 1, range(NT))
    emit_epilogue(8 * NP - 1)
    finish_a(NP - 1)
    drain_stats_mms(32)
    finish_b_act(NP - 1)
    finish_b_mm1(NP - 1)
    finish_b_mm2(NP - 1)
    finish_b(NP - 1)

    attn_pe = cnt["pe"]
    attn_dve = cnt["dve"]
    attn_act = cnt["act"]
    esB.close()

    g1 = strip1[:, 0:HCC * 1024]
    g2full = strip2
    g2 = g2full[:, 0:HCC * 1024]
    ln2b = g2full[:, 24 * 1024: 32 * 1024]

    esC = ExitStack()

    def sbC(name, shape, dtype):
        return esC.enter_context(nc.sbuf_tensor(name, shape, dtype))

    x2t = sbC("x2t", [128, CC * 1024], F32)
    wf = [sbC("wf%d" % i, [128, 2048], F32) for i in range(2)]
    wb = [sbC("wb0", [128, 4096], BF16), sbC("wb1", [128, 4096], BF16)]
    stmp = [sbC("stmp0", [128, 512], F32), sbC("stmp1", [128, 512], F32)]
    ysb = [sbC("ysb0", [128, 512], F32), sbC("ysb1", [128, 512], F32)]

    pW = [esC.enter_context(nc.psum_tensor("pW%d" % i, [128, 512], F32))
          for i in range(4)]

    pW_free = [("act", 0)] * 4
    wb_spill = [0, 0]
    wbf_done = (0, 0)
    wf_free = [0, 0]
    wfi = [0]
    wb_free = [0, 0]
    x2t_free = 0
    ysb_free = [0, 0]
    h2_pe_prev = 0

    def wf_load(src_ap, width=2048):
        s_ = wfi[0] % 2
        wfi[0] += 1
        dma(wf[s_][:, 0:width], src_ap, waits=[("dve", wf_free[s_])])
        return s_, cnt["ld"]

    for q4 in range(4):
        tok0 = 1024 * q4
        ls = ln2_store[2 * q4 + 1]
        gates = [("sln0", ls[0]), ("sln1", ls[1]), ("sln2", ls[2]),
                 ("sln3", ls[3]), ("act", attn_act if q4 == 0 else 0),
                 ("pe", attn_pe if q4 == 0 else h2_pe_prev)]
        dma(ln2b[:].rearrange("p (cc s) -> p cc s", cc=CC),
            ln2_s[:, tok0: tok0 + 1024].rearrange("(cc p) s -> p cc s", p=128),
            waits=gates)
        ln2b_ld = cnt["ld"]

        g1_done = 0
        x2_ld = 0
        for oc in range(HCC):
            i = oc % 2
            if oc == 2 or oc == 4:
                half = 0 if oc == 2 else 1
                dma(x2t[:, half * 4096:(half + 1) * 4096]
                    .rearrange("p (cc s) -> p cc s", cc=4),
                    x2_s[half * 512:(half + 1) * 512, tok0: tok0 + 1024]
                    .rearrange("(cc p) s -> p cc s", p=128),
                    waits=[("st", x2_store[2 * q4 + 1]), ("dve", x2t_free),
                           ("dve", attn_dve if q4 == 0 else 0)])
                x2_ld = cnt["ld"]
            if q4 == 0:
                s1_, l1 = wf_load(w1_t[oc], 1024)
                s2_, l2 = wf_load(w2_t[oc], 1024)
                dve("tensor_copy", wb[i][:, 0:1024], wf[s1_][:, 0:1024],
                    waits=[("ld", l1), ("pe", wb_free[i]),
                           ("swc%d" % i, wb_spill[i])], inc=False)
                wf_free[s1_] = cnt["dve"] + 1
                wf_free[s2_] = dve("tensor_copy", wb[i][:, 1024:2048],
                                   wf[s2_][:, 0:1024], waits=[("ld", l2)])
                wcast_d = cnt["dve"]
                dma(wbf_s[72 + oc, :, 0:2048], wb[i][:, 0:2048],
                    waits=[("dve", wcast_d)], kind="swc%d" % i, q="scalar")
                wb_spill[i] = cnt["swc%d" % i]
                wld = 0
            else:
                dma(wb[i][:, 0:2048], wbf_s[72 + oc, :, 0:2048],
                    waits=[("pe", wb_free[i]), ("swc0", wbf_done[0]),
                           ("swc1", wbf_done[1])])
                wld = cnt["ld"]
                wcast_d = 0
            for nt in range(2):
                sA_ = (2 * (oc * 2 + nt)) % 4
                sB_ = sA_ + 1
                for ccx in range(CC):
                    mm(pW[sA_][:], wb[i][:, ccx * 128: ccx * 128 + 128],
                       ln2b[:, ccx * 1024 + nt * 512: ccx * 1024 + nt * 512 + 512],
                       start=(ccx == 0), stop=(ccx == CC - 1),
                       waits=[("dve", wcast_d), ("ld", max(ln2b_ld, wld)),
                              pW_free[sA_]],
                       inc=(ccx == CC - 1))
                peA = cnt["pe"]
                for ccx in range(CC):
                    mm(pW[sB_][:], wb[i][:, 1024 + ccx * 128: 1024 + ccx * 128 + 128],
                       ln2b[:, ccx * 1024 + nt * 512: ccx * 1024 + nt * 512 + 512],
                       start=(ccx == 0), stop=(ccx == CC - 1),
                       waits=[pW_free[sB_]], inc=(ccx == CC - 1))
                peB = cnt["pe"]
                a_s = act(stmp[nt][:], pW[sA_][:], AF.Silu,
                          waits=[("pe", peA)], scale=s1_sb[:, oc:oc + 1])
                pW_free[sA_] = ("act", a_s)
                g1_done = dve("scalar_tensor_tensor",
                              g1[:, oc * 1024 + nt * 512: oc * 1024 + nt * 512 + 512],
                              pW[sB_][:], s2_sb[:, oc:oc + 1], stmp[nt][:],
                              ALU.mult, ALU.mult,
                              waits=[("pe", peB), ("act", a_s)])
                pW_free[sB_] = ("dve", g1_done)
            wb_free[i] = cnt["pe"]

        def h_stage(w_t, base, scol, src, dst, src_ready_dve, src_ready_act):
            done_act = 0
            for oc in range(HCC):
                i = oc % 2
                if q4 == 0:
                    for hfw in range(2):
                        s_, l_ = wf_load(w_t[oc, :, hfw * 2048:(hfw + 1) * 2048])
                        wf_free[s_] = dve(
                            "tensor_copy", wb[i][:, hfw * 2048:(hfw + 1) * 2048],
                            wf[s_][:],
                            waits=[("ld", l_), ("pe", wb_free[i]),
                                   ("swc%d" % i, wb_spill[i])])
                    wcast_d = cnt["dve"]
                    dma(wbf_s[base + oc], wb[i][:],
                        waits=[("dve", wcast_d)], kind="swc%d" % i, q="scalar")
                    wb_spill[i] = cnt["swc%d" % i]
                    wld = 0
                else:
                    dma(wb[i][:], wbf_s[base + oc],
                        waits=[("pe", wb_free[i]), ("swc0", wbf_done[0]),
                               ("swc1", wbf_done[1])])
                    wld = cnt["ld"]
                    wcast_d = 0
                for nt in range(2):
                    slot = (oc * 2 + nt) % 4
                    for ccx in range(HCC):
                        mm(pW[slot][:], wb[i][:, ccx * 128: ccx * 128 + 128],
                           src[:, ccx * 1024 + nt * 512: ccx * 1024 + nt * 512 + 512],
                           start=(ccx == 0), stop=(ccx == HCC - 1),
                           waits=[("dve", max(wcast_d, src_ready_dve)),
                                  ("act", src_ready_act), pW_free[slot],
                                  ("ld", wld)],
                           inc=(ccx == HCC - 1))
                    done_act = act(
                        dst[:, oc * 1024 + nt * 512: oc * 1024 + nt * 512 + 512],
                        pW[slot][:], AF.Gelu, waits=[("pe", cnt["pe"])],
                        scale=scol[:, oc:oc + 1])
                    pW_free[slot] = ("act", done_act)
                wb_free[i] = cnt["pe"]
            return done_act

        g2_done_act = h_stage(h1_t, 0, sh1_sb, g1, g2, g1_done, 0)
        h2_pe_start = cnt["pe"]
        g3_done_act = h_stage(h2_t, 32, sh2_sb, g2, g1, 0, g2_done_act)
        h2_pe_prev = cnt["pe"]

        for oc in range(CC):
            i = oc % 2
            if q4 == 0:
                for hfw in range(2):
                    s_, l_ = wf_load(w3_t[oc, :, hfw * 2048:(hfw + 1) * 2048])
                    wf_free[s_] = dve(
                        "tensor_copy", wb[i][:, hfw * 2048:(hfw + 1) * 2048],
                        wf[s_][:], waits=[("ld", l_), ("pe", wb_free[i]),
                                          ("swc%d" % i, wb_spill[i])])
                wcast_d = cnt["dve"]
                dma(wbf_s[64 + oc], wb[i][:], waits=[("dve", wcast_d)],
                    kind="swc%d" % i, q="scalar")
                wb_spill[i] = cnt["swc%d" % i]
                wld = 0
            else:
                dma(wb[i][:], wbf_s[64 + oc],
                    waits=[("pe", wb_free[i]), ("swc0", wbf_done[0]),
                           ("swc1", wbf_done[1])])
                wld = cnt["ld"]
                wcast_d = 0
            for nt in range(2):
                slot = (oc * 2 + nt) % 4
                for ccx in range(HCC):
                    mm(pW[slot][:], wb[i][:, ccx * 128: ccx * 128 + 128],
                       g1[:, ccx * 1024 + nt * 512: ccx * 1024 + nt * 512 + 512],
                       start=(ccx == 0), stop=(ccx == HCC - 1),
                       waits=[("dve", wcast_d), ("act", g3_done_act),
                              pW_free[slot], ("ld", wld)],
                       inc=(ccx == HCC - 1))
                syn = "sy%d" % nt
                d = dve("scalar_tensor_tensor", ysb[nt][:], pW[slot][:],
                        s3_sb[:, oc:oc + 1],
                        x2t[:, oc * 1024 + nt * 512: oc * 1024 + nt * 512 + 512],
                        ALU.mult, ALU.add,
                        waits=[("pe", cnt["pe"]), (syn, ysb_free[nt]),
                               ("ld", x2_ld)])
                pW_free[slot] = ("dve", d)
                x2t_free = d
                dma(yT[oc * 128:(oc + 1) * 128, tok0 + nt * 512: tok0 + nt * 512 + 512],
                    ysb[nt][:], waits=[("dve", d)], kind=syn, q="scalar")
                ysb_free[nt] = cnt[syn]
            wb_free[i] = cnt["pe"]
        if q4 == 0:
            wbf_done = (cnt["swc0"], cnt["swc1"])

    final_waits = [(nm, cnt[nm]) for nm in
                   ("st", "srm", "sq0", "sq1", "sv0", "sv1", "sln0",
                    "sln1", "sln2", "sln3", "swc0", "swc1", "sy0", "sy1")]
    emit("sync", final_waits, lambda e: e.nop())

    esC.close()

    with ExitStack() as semstack:
        for nm in ("ld", "st", "pe", "act", "dve", "srm", "sq0", "sq1",
                   "sv0", "sv1", "sln0", "sln1", "sln2", "sln3", "swc0",
                   "swc1", "sy0", "sy1", "lx0", "lx1", "lx2", "lx3",
                   "lw0", "lw1", "lqk"):
            sems[nm] = semstack.enter_context(nc.semaphore("s_" + nm))
        with nc.Block() as block:
            @block.sync
            def _(e):
                for f in ops["sync"]:
                    f(e)

            @block.tensor
            def _(e):
                for f in ops["tensor"]:
                    f(e)

            @block.scalar
            def _(e):
                for f in ops["scalar"]:
                    f(e)

            @block.vector
            def _(e):
                for f in ops["vector"]:
                    f(e)
    stack.close()
    return nc


def _tile_w(W, n_oc, n_cc):
    T = W.T.reshape(n_cc, 128, n_oc, 128)
    return np.ascontiguousarray(T.transpose(2, 1, 0, 3).reshape(n_oc, 128, n_cc * 128))


def kernel(L0_attn_queries, L0_attn_keys, L0_attn_values,
           L0_attn_Wq_w, L0_attn_Wq_b, L0_attn_Wk_w, L0_attn_Wk_b,
           L0_attn_Wv_w, L0_attn_Wv_b,
           L0_ffn_w1_w, L0_ffn_w1_s, L0_ffn_w2_w, L0_ffn_w2_s,
           L0_ffn_h1_w, L0_ffn_h1_s, L0_ffn_h2_w, L0_ffn_h2_s,
           L0_ffn_w3_w, L0_ffn_w3_s):
    f = lambda a: np.asarray(a, dtype=np.float32)
    x = f(L0_attn_queries)
    keys = f(L0_attn_keys)[0]
    values = f(L0_attn_values)[0]

    shared = {
        "keysT": np.ascontiguousarray(keys.T),
        "valuesT": np.ascontiguousarray(values.T),
        "wq_t": _tile_w(f(L0_attn_Wq_w), 8, 8),
        "wk_t": _tile_w(f(L0_attn_Wk_w), 8, 8),
        "wv_io": np.ascontiguousarray(f(L0_attn_Wv_w).T),
        "bq_t": np.ascontiguousarray(f(L0_attn_Wq_b).reshape(8, 128).T),
        "bk_t": np.ascontiguousarray(f(L0_attn_Wk_b).reshape(8, 128).T),
        "bv_row": np.ascontiguousarray(f(L0_attn_Wv_b).reshape(1, 1024)),
        "w1_t": _tile_w(f(L0_ffn_w1_w), 32, 8),
        "w2_t": _tile_w(f(L0_ffn_w2_w), 32, 8),
        "h1_t": _tile_w(f(L0_ffn_h1_w), 32, 32),
        "h2_t": _tile_w(f(L0_ffn_h2_w), 32, 32),
        "w3_t": _tile_w(f(L0_ffn_w3_w), 8, 32),
        "s1_t": np.ascontiguousarray(f(L0_ffn_w1_s).reshape(32, 128).T),
        "s2_t": np.ascontiguousarray(f(L0_ffn_w2_s).reshape(32, 128).T),
        "sh1_t": np.ascontiguousarray(f(L0_ffn_h1_s).reshape(32, 128).T),
        "sh2_t": np.ascontiguousarray(f(L0_ffn_h2_s).reshape(32, 128).T),
        "s3_t": np.ascontiguousarray(f(L0_ffn_w3_s).reshape(8, 128).T),
        "ident": np.eye(128, dtype=np.float32),
    }
    in_maps = []
    for b in range(8):
        m = dict(shared)
        m["xT"] = np.ascontiguousarray(x[b].T)
        in_maps.append(m)

    nc = build_program()
    res = run_bass_kernel_spmd(nc, in_maps, core_ids=list(range(8)))
    out = np.empty((8, 4096, 1024), dtype=np.float32)
    for b in range(8):
        out[b] = np.asarray(res.results[b]["yT"], dtype=np.float32).T
    return out



# version 39
# speedup vs baseline: 1.0157x; 1.0157x over previous
import numpy as np
from contextlib import ExitStack

import concourse.bass as bass
import concourse.mybir as mybir
from concourse.bass_utils import run_bass_kernel_spmd

F32 = mybir.dt.float32
BF16 = mybir.dt.bfloat16
AF = mybir.ActivationFunctionType
ALU = mybir.AluOpType

S = 4096
D = 1024
CC = 8
NT = 32
NP = 8
DHID = 4096
HCC = 32


def build_program():
    nc = bass.Bass("TRN2", target_bir_lowering=False)

    xT = nc.dram_tensor("xT", [D, S], F32, kind="ExternalInput").ap()
    keysT = nc.dram_tensor("keysT", [D, S], F32, kind="ExternalInput").ap()
    valuesT = nc.dram_tensor("valuesT", [D, S], F32, kind="ExternalInput").ap()
    wq_t = nc.dram_tensor("wq_t", [8, 128, 1024], F32, kind="ExternalInput").ap()
    wk_t = nc.dram_tensor("wk_t", [8, 128, 1024], F32, kind="ExternalInput").ap()
    wv_io = nc.dram_tensor("wv_io", [D, D], F32, kind="ExternalInput").ap()
    bq_t = nc.dram_tensor("bq_t", [128, 8], F32, kind="ExternalInput").ap()
    bk_t = nc.dram_tensor("bk_t", [128, 8], F32, kind="ExternalInput").ap()
    bv_row = nc.dram_tensor("bv_row", [1, D], F32, kind="ExternalInput").ap()
    w1_t = nc.dram_tensor("w1_t", [32, 128, 1024], F32, kind="ExternalInput").ap()
    w2_t = nc.dram_tensor("w2_t", [32, 128, 1024], F32, kind="ExternalInput").ap()
    h1_t = nc.dram_tensor("h1_t", [32, 128, 4096], F32, kind="ExternalInput").ap()
    h2_t = nc.dram_tensor("h2_t", [32, 128, 4096], F32, kind="ExternalInput").ap()
    w3_t = nc.dram_tensor("w3_t", [8, 128, 4096], F32, kind="ExternalInput").ap()
    s1_t = nc.dram_tensor("s1_t", [128, 32], F32, kind="ExternalInput").ap()
    s2_t = nc.dram_tensor("s2_t", [128, 32], F32, kind="ExternalInput").ap()
    sh1_t = nc.dram_tensor("sh1_t", [128, 32], F32, kind="ExternalInput").ap()
    sh2_t = nc.dram_tensor("sh2_t", [128, 32], F32, kind="ExternalInput").ap()
    s3_t = nc.dram_tensor("s3_t", [128, 8], F32, kind="ExternalInput").ap()
    ident = nc.dram_tensor("ident", [128, 128], F32, kind="ExternalInput").ap()

    yT = nc.dram_tensor("yT", [D, S], F32, kind="ExternalOutput").ap()

    qT_s = nc.dram_tensor("qT_s", [8, 128, S], BF16, kind="Internal").ap()
    kT_s = nc.dram_tensor("kT_s", [8, 128, S], BF16, kind="Internal").ap()
    v_s = nc.dram_tensor("v_s", [NT, 128, D], BF16, kind="Internal").ap()
    x2_s = nc.dram_tensor("x2_s", [D, S], F32, kind="Internal").ap()
    ln2_s = nc.dram_tensor("ln2_s", [D, S], BF16, kind="Internal").ap()
    wbf_s = nc.dram_tensor("wbf_s", [104, 128, 4096], BF16, kind="Internal").ap()
    m_row_s = nc.dram_tensor("m_row_s", [1, S], F32, kind="Internal").ap()
    r_row_s = nc.dram_tensor("r_row_s", [1, S], F32, kind="Internal").ap()

    ops = {"sync": [], "tensor": [], "scalar": [], "vector": []}
    cnt = {"ld": 0, "st": 0, "pe": 0, "act": 0, "dve": 0,
           "srm": 0, "sq0": 0, "sq1": 0, "sv0": 0, "sv1": 0,
           "sln0": 0, "sln1": 0, "sln2": 0, "sln3": 0,
           "swc0": 0, "swc1": 0, "sy0": 0, "sy1": 0,
           "lx0": 0, "lx1": 0, "lx2": 0, "lx3": 0,
           "lw0": 0, "lw1": 0, "lqk": 0}
    hwm = {}
    sems = {}

    def emit(eng, waits, make, incs=()):
        rw = []
        for sname, v in waits:
            if v is None or v <= 0:
                continue
            key = (eng, sname)
            if hwm.get(key, 0) >= v:
                continue
            hwm[key] = v
            rw.append((sname, v))
        for sname, n in incs:
            cnt[sname] += n
        mine = tuple(incs)

        def run(e, rw=tuple(rw), mk=make, inc=mine):
            for sname, v in rw:
                e.wait_ge(sems[sname], v)
            r = mk(e)
            for sname, n in inc:
                r = r.then_inc(sems[sname], n)
        ops[eng].append(run)

    def dma(out, in_, waits=(), kind="ld", q="sync"):
        emit(q, waits, (lambda e, o=out, i=in_: e.dma_start(out=o, in_=i)),
             [(kind, 16)])
        return cnt[kind]

    def mm(out, lhsT, rhs, start, stop, waits=(), inc=False):
        emit("tensor", waits,
             (lambda e, o=out, l=lhsT, r=rhs, a=start, b=stop:
              e.matmul(o, l, r, start=a, stop=b)),
             [("pe", 1)] if inc else ())
        return cnt["pe"]

    def tr(out, in_, idn, waits=(), inc=True):
        emit("tensor", waits,
             (lambda e, o=out, i=in_, d=idn: e.transpose(o, i, d)),
             [("pe", 1)] if inc else ())
        return cnt["pe"]

    def act(out, in_, func, waits=(), bias=None, scale=None, inc=True):
        kw = {}
        if bias is not None:
            kw["bias"] = bias
        if scale is not None:
            kw["scale"] = scale
        emit("scalar", waits,
             (lambda e, o=out, i=in_, f=func, k=kw: e.activation(o, i, f, **k)),
             [("act", 1)] if inc else ())
        return cnt["act"]

    def dve(name, *args, waits=(), inc=True):
        emit("vector", waits,
             (lambda e, n=name, a=args: getattr(e, n)(*a)),
             [("dve", 1)] if inc else ())
        return cnt["dve"]

    stack = ExitStack()

    def sb(name, shape, dtype):
        return stack.enter_context(nc.sbuf_tensor(name, shape, dtype))

    ident32 = sb("ident32", [128, 128], F32)
    onesrow = sb("onesrow", [1, 128], F32)
    onescol = sb("onescol", [128, 1], BF16)
    eps1 = sb("eps1", [1, 1], F32)
    s1_sb = sb("s1_sb", [128, 32], F32)
    s2_sb = sb("s2_sb", [128, 32], F32)
    sh1_sb = sb("sh1_sb", [128, 32], F32)
    sh2_sb = sb("sh2_sb", [128, 32], F32)
    s3_sb = sb("s3_sb", [128, 8], F32)

    strip1 = sb("strip1", [128, CC * S], BF16)
    strip2 = sb("strip2", [128, CC * S], BF16)
    m_tmp2 = sb("m_tmp2", [1, 512], F32)
    r_tmp2 = sb("r_tmp2", [1, 512], F32)

    dma(ident32[:], ident)
    dma(s1_sb[:], s1_t)
    dma(s2_sb[:], s2_t)
    dma(sh1_sb[:], sh1_t)
    dma(sh2_sb[:], sh2_t)
    dma(s3_sb[:], s3_t)
    dve("memset", onesrow[:], 1.0)
    dve("memset", onescol[:], 1.0 / 1024.0, inc=False)
    dve("memset", eps1[:], 1e-5, inc=False)

    esA = ExitStack()

    def sbA(name, shape, dtype):
        return esA.enter_context(nc.sbuf_tensor(name, shape, dtype))

    bq_sb = sbA("bq_sb", [128, 8], F32)
    bk_sb = sbA("bk_sb", [128, 8], F32)
    bv_sb = sbA("bv_sb", [1, D], F32)
    dma(bq_sb[:], bq_t)
    dma(bk_sb[:], bk_t)
    dma(bv_sb[:], bv_row)
    ld_setup = cnt["ld"]

    xf = [sbA("xf%d" % i, [128, 1024], F32) for i in range(4)]
    xf_free = [[] for _ in range(4)]
    for cc in range(CC):
        for q4 in range(4):
            i = q4 % 4
            lxn = "lx%d" % i
            dma(xf[i][:], xT[cc * 128:(cc + 1) * 128, q4 * 1024:(q4 + 1) * 1024],
                waits=xf_free[i], kind=lxn)
            off = cc * S + q4 * 1024
            emit("scalar", [(lxn, cnt[lxn])],
                 (lambda e, o=strip1[:, off:off + 1024], s=xf[i][:]:
                  e.copy(o, s)), [("act", 1)])
            csq = dve("tensor_tensor", strip2[:, off:off + 1024], xf[i][:],
                      xf[i][:], ALU.mult, waits=[(lxn, cnt[lxn])])
            xf_free[i] = [("act", cnt["act"]), ("dve", csq)]
    cast_done = ("act", cnt["act"])
    cast_sq_done = ("dve", cnt["dve"])

    pm = [esA.enter_context(nc.psum_tensor("pm0", [1, 512], F32)),
          esA.enter_context(nc.psum_tensor("pm1", [1, 512], F32))]
    pq = [esA.enter_context(nc.psum_tensor("pq0", [1, 512], F32)),
          esA.enter_context(nc.psum_tensor("pq1", [1, 512], F32))]
    pbb = esA.enter_context(nc.psum_tensor("pbb", [128, 512], F32))
    m_tmp = sbA("m_tmp", [1, 512], F32)
    r_tmp = sbA("r_tmp", [1, 512], F32)
    varrow = sbA("varrow", [1, 512], F32)
    msq = sbA("msq", [1, 512], F32)
    mBst = sbA("mBst", [128, 512], F32)
    rstdBst = sbA("rstdBst", [128, 512], F32)
    t1scr = sbA("t1scr", [128, 512], F32)

    stat_free = [0, 0]
    row_store = 0
    pbb_free = 0
    lnxb_done = 0
    lnxb_st = [0] * 8
    pe_stats = [0] * 8

    def a0_stats(st):
        par = st % 2
        for cc in range(CC):
            mm(pm[par][:], onescol[:],
               strip1[:, cc * S + st * 512: cc * S + st * 512 + 512],
               start=(cc == 0), stop=(cc == CC - 1),
               waits=[cast_done, ("dve", stat_free[par])])
        for cc in range(CC):
            mm(pq[par][:], onescol[:],
               strip2[:, cc * S + st * 512: cc * S + st * 512 + 512],
               start=(cc == 0), stop=(cc == CC - 1),
               waits=[cast_sq_done] if cc == 0 else (), inc=(cc == CC - 1))
        pe_stats[st] = cnt["pe"]

    def a0_rows_lnxb(st):
        nonlocal row_store, pbb_free, lnxb_done
        par = st % 2
        dve("tensor_copy", m_tmp[:], pm[par][:],
            waits=[("pe", pe_stats[st]), ("srm", row_store)], inc=False)
        dve("tensor_tensor", msq[:], m_tmp[:], m_tmp[:], ALU.mult, inc=False)
        stat_free[par] = dve("tensor_tensor", varrow[:], pq[par][:], msq[:],
                             ALU.subtract)
        m_evac = stat_free[par]
        act(varrow[:], varrow[:], AF.Ln, waits=[("dve", stat_free[par])],
            bias=eps1[:], inc=False)
        rstd_st = act(r_tmp[:], varrow[:], AF.Exp, scale=-0.5,
                      waits=[("srm", row_store)])
        mm(pbb[:], onesrow[:], m_tmp[:], True, True,
           waits=[("dve", max(pbb_free, m_evac))], inc=True)
        dve("tensor_copy", mBst[:], pbb[:], waits=[("pe", cnt["pe"])])
        mm(pbb[:], onesrow[:], r_tmp[:], True, True,
           waits=[("dve", cnt["dve"]), ("act", rstd_st)], inc=True)
        bcast_pe = cnt["pe"]
        pbb_free = dve("tensor_copy", rstdBst[:], pbb[:], waits=[("pe", bcast_pe)])
        dma(m_row_s[0:1, st * 512:(st + 1) * 512], m_tmp[:],
            waits=[("dve", m_evac), ("pe", bcast_pe)], kind="srm", q="scalar")
        dma(r_row_s[0:1, st * 512:(st + 1) * 512], r_tmp[:],
            waits=[("act", rstd_st)], kind="srm", q="scalar")
        row_store = cnt["srm"]
        for cc in range(CC):
            sl = strip1[:, cc * S + st * 512: cc * S + st * 512 + 512]
            dve("tensor_copy", t1scr[:], sl, inc=False)
            dve("tensor_tensor", t1scr[:], t1scr[:], mBst[:], ALU.subtract, inc=False)
            lnxb_done = dve("tensor_tensor", sl, t1scr[:], rstdBst[:], ALU.mult)
        lnxb_st[st] = lnxb_done

    for st in range(8):
        a0_stats(st)
        if st > 0:
            a0_rows_lnxb(st - 1)
    a0_rows_lnxb(7)

    wob = [sbA("wob0", [128, 1024], F32), sbA("wob1", [128, 1024], F32)]
    wpb = [sbA("wpb0", [128, 1024], BF16), sbA("wpb1", [128, 1024], BF16)]
    wvb = sbA("wvb", [128, 8 * 1024], BF16)
    qout = [sbA("qout0", [128, 512], BF16), sbA("qout1", [128, 512], BF16)]
    pproj = [esA.enter_context(nc.psum_tensor("pproj0", [128, 512], F32)),
             esA.enter_context(nc.psum_tensor("pproj1", [128, 512], F32))]

    wob_free = [0, 0]
    wpb_free = [0, 0]
    qout_free = [0, 0]
    pslot_free = [0, 0]
    qk_store = {"q": [(0, 0)] * 8, "k": [(0, 0)] * 8}
    q_pe_done = 0

    def proj_stage(nm, w_src, bias_sb, rhs_strip, rhs_ready, out_dram, store_list):
        for oc in range(8):
            i = oc % 2
            lwn = "lw%d" % i
            dma(wob[i][:], w_src[oc], waits=[("dve", wob_free[i])], kind=lwn)
            wob_free[i] = dve("tensor_copy", wpb[i][:], wob[i][:],
                              waits=[(lwn, cnt[lwn]), ("pe", wpb_free[i])])
            wcast = cnt["dve"]
            for st in range(8):
                slot = (oc * 8 + st) % 2
                rdy = rhs_ready[st] if isinstance(rhs_ready, list) else rhs_ready
                for cc in range(CC):
                    mm(pproj[slot][:],
                       wpb[i][:, cc * 128: cc * 128 + 128],
                       rhs_strip[:, cc * S + st * 512: cc * S + st * 512 + 512],
                       start=(cc == 0), stop=(cc == CC - 1),
                       waits=[("dve", max(rdy, wcast)),
                              ("act", pslot_free[slot])],
                       inc=(cc == CC - 1))
                pslot_free[slot] = act(
                    qout[st % 2][:], pproj[slot][:], AF.Relu,
                    waits=[("pe", cnt["pe"]), ("ld", ld_setup),
                           ("sq%d" % (st % 2), qout_free[st % 2])],
                    bias=bias_sb[:, oc:oc + 1])
                sqn = "sq%d" % (st % 2)
                dma(out_dram[oc, :, st * 512:(st + 1) * 512], qout[st % 2][:],
                    waits=[("act", cnt["act"])], kind=sqn, q="scalar")
                qout_free[st % 2] = cnt[sqn]
            wpb_free[i] = cnt["pe"]
            store_list[oc] = (cnt["sq0"], cnt["sq1"])

    proj_stage("q", wq_t, bq_sb, strip1, lnxb_st, qT_s, qk_store["q"])
    q_pe_done = cnt["pe"]

    keys_q = [0] * 4
    for q4 in range(4):
        for cc in range(CC):
            i = cc % 4
            lxn = "lx%d" % i
            dma(xf[i][:], keysT[cc * 128:(cc + 1) * 128, q4 * 1024:(q4 + 1) * 1024],
                waits=xf_free[i], kind=lxn)
            off = cc * S + q4 * 1024
            xf_free[i] = [("dve", dve("tensor_copy", strip2[:, off:off + 1024],
                                      xf[i][:],
                                      waits=[(lxn, cnt[lxn]),
                                             ("pe", pe_stats[7])]))]
        keys_q[q4] = cnt["dve"]
    keys_st = [keys_q[st // 2] for st in range(8)]

    proj_stage("k", wk_t, bk_sb, strip2, keys_st, kT_s, qk_store["k"])

    wv_cast = 0
    for ccw in range(8):
        i = ccw % 2
        lwn = "lw%d" % i
        dma(wob[i][:], wv_io[ccw * 128:(ccw + 1) * 128, :],
            waits=[("dve", wob_free[i])], kind=lwn)
        wob_free[i] = dve("tensor_copy", wvb[:, ccw * 1024:(ccw + 1) * 1024],
                          wob[i][:], waits=[(lwn, cnt[lwn]), ("pe", wpb_free[i])])
        wv_cast = cnt["dve"]

    bvB = sbA("bvB", [128, 1024], F32)
    mm(pbb[:], onesrow[:], bv_sb[0:1, 0:512], True, True,
       waits=[("dve", pbb_free), ("ld", ld_setup)], inc=True)
    dve("tensor_copy", bvB[:, 0:512], pbb[:], waits=[("pe", cnt["pe"])])
    mm(pbb[:], onesrow[:], bv_sb[0:1, 512:1024], True, True,
       waits=[("dve", cnt["dve"])], inc=True)
    pbb_free = dve("tensor_copy", bvB[:, 512:1024], pbb[:], waits=[("pe", cnt["pe"])])

    vals_q = [0] * 4
    for q4 in range(4):
        for cc in range(CC):
            i = cc % 4
            lxn = "lx%d" % i
            dma(xf[i][:], valuesT[cc * 128:(cc + 1) * 128, q4 * 1024:(q4 + 1) * 1024],
                waits=[*xf_free[i], ("pe", q_pe_done)], kind=lxn)
            off = cc * S + q4 * 1024
            xf_free[i] = [("dve", dve("tensor_copy", strip1[:, off:off + 1024],
                                      xf[i][:], waits=[(lxn, cnt[lxn])]))]
        vals_q[q4] = cnt["dve"]

    vout = [sbA("vout0", [128, 512], BF16), sbA("vout1", [128, 512], BF16)]
    vscr = sbA("vscr", [128, 512], F32)
    vout_free = [0, 0]
    vslot_free = [0, 0]
    v_store = (0, 0)
    for tc in range(NT):
        for ot in range(2):
            slot = (tc * 2 + ot) % 2
            for cc in range(CC):
                mm(pproj[slot][:],
                   strip1[:, cc * S + tc * 128: cc * S + tc * 128 + 128],
                   wvb[:, cc * 1024 + ot * 512: cc * 1024 + ot * 512 + 512],
                   start=(cc == 0), stop=(cc == CC - 1),
                   waits=[("dve", max(vals_q[tc // 8], wv_cast, pbb_free)),
                          ("dve", vslot_free[slot]),
                          ("act", pslot_free[slot])],
                   inc=(cc == CC - 1))
            dve("tensor_tensor", vscr[:], pproj[slot][:],
                bvB[:, ot * 512:(ot + 1) * 512], ALU.add,
                waits=[("pe", cnt["pe"]), ("sv%d" % slot, vout_free[slot])],
                inc=False)
            vslot_free[slot] = dve("tensor_scalar_max", vout[slot][:], vscr[:], 0.0)
            svn = "sv%d" % slot
            dma(v_s[tc, :, ot * 512:(ot + 1) * 512], vout[slot][:],
                waits=[("dve", cnt["dve"])], kind=svn, q="scalar")
            vout_free[slot] = cnt[svn]
            v_store = (cnt["sv0"], cnt["sv1"])
    a1_pe = cnt["pe"]
    a1_dve = cnt["dve"]

    esA.close()
    esB = ExitStack()

    def sbB(name, shape, dtype):
        return esB.enter_context(nc.sbuf_tensor(name, shape, dtype))

    qp = [strip1[:, 0:S], strip1[:, S:2 * S]]
    kz = [[strip1[:, 2 * S:3 * S], strip1[:, 3 * S:4 * S]],
          [strip1[:, 4 * S:5 * S], strip1[:, 5 * S:6 * S]]]
    P_all = strip2[:, 0:2 * NT * 512]

    xcol = [sbB("xcol0", [128, CC * 512], F32), sbB("xcol1", [128, CC * 512], F32)]
    va = [sbB("va0", [128, NT * 130], BF16), sbB("va1", [128, NT * 130], BF16)]
    mBp = sbB("mBp", [128, 512], F32)
    rstdBp = sbB("rstdBp", [128, 512], F32)
    saT = [sbB("saT0", [65, 512], F32), sbB("saT1", [65, 512], F32)]
    ctmp = sbB("ctmp", [128, 64], F32)
    srec = sbB("srec", [128, 1], F32)
    t2scr = sbB("t2scr", [128, 512], F32)
    scr2 = [strip1[:, 6 * S + i * 512: 6 * S + (i + 1) * 512] for i in range(CC)]
    ln2out = [strip1[:, 7 * S + i * 512: 7 * S + (i + 1) * 512] for i in range(4)]
    m2t = sbB("m2t", [1, 512], F32)
    v2t = sbB("v2t", [1, 512], F32)
    msq2t = sbB("msq2t", [1, 512], F32)
    mB2a = sbB("mB2a", [128, 512], F32)
    rB2a = sbB("rB2a", [128, 512], F32)

    pS = [esB.enter_context(nc.psum_tensor("pS0", [128, 1024], F32)),
          esB.enter_context(nc.psum_tensor("pS1", [128, 1024], F32))]
    pAV = [esB.enter_context(nc.psum_tensor("pAV0", [65, 512], F32)),
           esB.enter_context(nc.psum_tensor("pAV1", [65, 512], F32))]
    pTR = esB.enter_context(nc.psum_tensor("pTR", [128, 260], F32))
    pbb2 = esB.enter_context(nc.psum_tensor("pbb2", [128, 512], F32))

    va_ones = [0, 0]
    for i in range(2):
        va_ones[i] = dve(
            "memset",
            va[i][:].rearrange("p (g c) -> p g c", c=65)[:, :, 64:65], 1.0)
    kz_zero = 0
    for i in range(2):
        dve("memset", kz[i][0][64:128, :], 0.0, waits=[("pe", a1_pe)], inc=False)
        kz_zero = dve("memset", kz[i][1][0:64, :], 0.0)

    pair_qk_ld = [0] * (NP + 1)
    pair_xc_ld = [0] * (NP + 1)

    def prefetch_qk(p, pe_free):
        buf = p % 2
        dma(qp[buf], qT_s[p], waits=[("pe", max(pe_free, a1_pe)),
                                     ("dve", max(a1_dve, kz_zero)),
                                     ("sq0", qk_store["q"][p][0]),
                                     ("sq1", qk_store["q"][p][1])], kind="lqk")
        dma(kz[buf][0][0:64, :], kT_s[p, 0:64, :],
            waits=[("sq0", qk_store["k"][p][0]),
                   ("sq1", qk_store["k"][p][1])], kind="lqk")
        dma(kz[buf][1][64:128, :], kT_s[p, 64:128, :], kind="lqk")
        pair_qk_ld[p] = cnt["lqk"]

    def prefetch_rest(p, dve_free):
        buf = p % 2
        dma(xcol[buf][:].rearrange("p (cc s) -> p cc s", cc=CC),
            xT[:, 512 * p: 512 * p + 512].rearrange("(cc p) s -> p cc s", p=128),
            waits=[("dve", dve_free)])
        for h in (0, 1):
            dst = va[buf][:].rearrange("p (tc g) -> p tc g", g=130)[:, :, 65 * h: 65 * h + 64]
            srcv = v_s[:, :, 128 * p + 64 * h: 128 * p + 64 * h + 64] \
                .rearrange("tc p d -> p tc d")
            dma(dst, srcv, waits=[("sv0", v_store[0]), ("sv1", v_store[1]),
                                  ("dve", va_ones[buf])])
        return cnt["ld"]

    def prefetch_pair(p, pe_free, dve_free):
        prefetch_qk(p, pe_free)
        return prefetch_rest(p, dve_free)

    pair_pe_done = [0] * (NP + 2)
    pair_ld = [0] * (NP + 1)
    x2_store = [0] * (NP + 2)
    ln2_store = [(0, 0, 0, 0)] * (NP + 2)
    xcol_free = [0] * (NP + 2)

    exp_cnt = {}
    av_prog = {}
    sa_copy = {0: 0, 1: 0}
    tr_done = {0: 0, 1: 0}
    blk_done = {0: 0, 1: 0}
    pbb2_free = 0
    mB_used = 0
    bcast_pe2 = 0
    scr2_free = 0
    ln2out_free = [0, 0, 0, 0]
    v2_done = [0]

    def emit_av_chunks(g, tcs):
        pp, pj = g // 8, g % 8
        pbuf = pp % 2
        for h in (0, 1):
            for tc in tcs:
                mm(pAV[h][:],
                   va[pbuf][:, 130 * tc + 65 * h: 130 * tc + 65 * h + 65],
                   P_all[:, 1024 * tc + 512 * h: 1024 * tc + 512 * h + 512],
                   start=(tc == 0), stop=(tc == NT - 1),
                   waits=[("act", exp_cnt[(pj, tc)]),
                          ("dve", sa_copy[h]),
                          ("ld", max(pair_ld[pp], rows_ld_d[pp]))],
                   inc=True)
                av_prog[(h, pj, tc)] = cnt["pe"]

    def emit_epilogue_a(g):
        pp, pj = g // 8, g % 8
        for h in (0, 1):
            sa_copy[h] = dve("tensor_copy", saT[h][:], pAV[h][:],
                             waits=[("pe", av_prog[(h, pj, NT - 1)]),
                                    ("pe", tr_done[h])])

    def emit_epilogue_b(g, h):
        pp, pj = g // 8, g % 8
        pbuf = pp % 2
        for b in range(4):
            tr(pTR[:, 65 * b: 65 * b + 65], saT[h][0:65, 128 * b: 128 * b + 128],
               ident32[0:65, 0:65],
               waits=[("dve", sa_copy[h]),
                      ("dve", blk_done[0]), ("dve", blk_done[1])],
               inc=(b == 3))
        tr_done[h] = cnt["pe"]
        for b in range(4):
            dve("reciprocal", srec[:], pTR[:, 65 * b + 64: 65 * b + 65],
                waits=[("pe", tr_done[h])], inc=False)
            dve("tensor_copy", ctmp[:], pTR[:, 65 * b: 65 * b + 64], inc=False)
            dve("tensor_scalar_mul", ctmp[:], ctmp[:], srec[:], inc=False)
            col0 = 256 * h + (pj // 2)
            ccx = 4 * (pj % 2) + b
            slx = xcol[pbuf][:, ccx * 512 + col0: ccx * 512 + col0 + 253: 4]
            blk_done[h] = dve("tensor_tensor", slx, slx, ctmp[:], ALU.add)
        if pj == 7 and h == 1:
            pair_pe_done[pp] = cnt["pe"]

    def emit_epilogue(g):
        emit_epilogue_a(g)
        emit_epilogue_b(g, 0)
        emit_epilogue_b(g, 1)

    stats_mm_q = []

    def finish_a(pp):
        nonlocal scr2_free, pbb2_free
        pbuf = pp % 2
        dma(x2_s[:, 512 * pp: 512 * pp + 512].rearrange("(cc p) s -> p cc s", p=128),
            xcol[pbuf][:].rearrange("p (cc s) -> p cc s", cc=CC),
            waits=[("dve", blk_done[0]), ("dve", blk_done[1])], kind="st")
        x2_store[pp] = cnt["st"]
        for ccx in range(CC):
            sl = xcol[pbuf][:, ccx * 512:(ccx + 1) * 512]
            cd = dve("tensor_copy", scr2[ccx][:], sl, waits=[("pe", scr2_free)])
            stats_mm_q.append((0, ccx, cd, sl))

    def drain_stats_mms(nmax):
        nonlocal scr2_free, pbb2_free
        for _ in range(nmax):
            if not stats_mm_q:
                return
            sq, ccx, cd, sl = stats_mm_q.pop(0)
            mm(pbb2[0:1, :], onescol[:], scr2[ccx][:],
               start=(ccx == 0), stop=(ccx == CC - 1),
               waits=[("dve", cd), ("dve", pbb2_free)], inc=True)
            if sq == 0:
                cd2 = dve("tensor_tensor", scr2[ccx][:], sl, sl, ALU.mult,
                          waits=[("pe", cnt["pe"])])
                stats_mm_q.append((1, ccx, cd2, sl))
            if ccx == CC - 1:
                row = v2t if sq else m2t
                pbb2_free = dve("tensor_copy", row[:], pbb2[0:1, :],
                                waits=[("pe", cnt["pe"])])
                if sq:
                    scr2_free = cnt["pe"]
                    dve("tensor_tensor", msq2t[:], m2t[:], m2t[:], ALU.mult,
                        inc=False)
                    v2_done[0] = dve("tensor_tensor", v2t[:], v2t[:], msq2t[:],
                                     ALU.subtract)

    rstd_act = [0]

    def finish_b_act(pp):
        act(v2t[:], v2t[:], AF.Ln, waits=[("dve", v2_done[0])],
            bias=eps1[:], inc=False)
        rstd_act[0] = act(v2t[:], v2t[:], AF.Exp, scale=-0.5)

    def finish_b_mm1(pp):
        mm(pbb2[:], onesrow[:], m2t[:], True, True,
           waits=[("dve", pbb2_free)], inc=True)
        dve("tensor_copy", mB2a[:], pbb2[:], waits=[("pe", cnt["pe"])])

    def finish_b_mm2(pp):
        nonlocal pbb2_free
        mm(pbb2[:], onesrow[:], v2t[:], True, True,
           waits=[("dve", cnt["dve"]), ("act", rstd_act[0])], inc=True)
        pbb2_free = dve("tensor_copy", rB2a[:], pbb2[:], waits=[("pe", cnt["pe"])])

    def finish_b(pp, ccxs=range(CC)):
        pbuf = pp % 2
        for ccx in ccxs:
            sl = xcol[pbuf][:, ccx * 512:(ccx + 1) * 512]
            slnn = "sln%d" % (ccx % 4)
            dve("tensor_tensor", t2scr[:], sl, mB2a[:], ALU.subtract, inc=False)
            lnd = dve("tensor_tensor", ln2out[ccx % 4][:], t2scr[:], rB2a[:],
                      ALU.mult, waits=[(slnn, ln2out_free[ccx % 4])])
            dma(ln2_s[ccx * 128:(ccx + 1) * 128, 512 * pp: 512 * pp + 512],
                ln2out[ccx % 4][:], waits=[("dve", lnd)], kind=slnn, q="scalar")
            ln2out_free[ccx % 4] = cnt[slnn]
        if ccxs[-1] == CC - 1:
            ln2_store[pp] = (cnt["sln0"], cnt["sln1"], cnt["sln2"], cnt["sln3"])
            xcol_free[pp] = cnt["dve"]

    rows_ld_d = [0] * (NP + 1)

    def load_rows(pp):
        dma(m_tmp2[:], m_row_s[0:1, 512 * pp: 512 * pp + 512],
            waits=[("srm", row_store), ("pe", bcast_pe2)])
        dma(r_tmp2[:], r_row_s[0:1, 512 * pp: 512 * pp + 512])
        rows_ld_d[pp] = cnt["ld"]

    def pair_top(pp):
        nonlocal bcast_pe2, pbb2_free, mB_used
        pbuf = pp % 2
        if not rows_ld_d[pp]:
            load_rows(pp)
        mm(pbb2[:], onesrow[:], m_tmp2[:], True, True,
           waits=[("dve", max(pbb2_free, mB_used)), ("ld", rows_ld_d[pp])],
           inc=True)
        dve("tensor_copy", mBp[:], pbb2[:], waits=[("pe", cnt["pe"])])
        mm(pbb2[:], onesrow[:], r_tmp2[:], True, True,
           waits=[("dve", cnt["dve"])], inc=True)
        bcast_pe2 = cnt["pe"]
        pbb2_free = dve("tensor_copy", rstdBp[:], pbb2[:],
                        waits=[("pe", bcast_pe2)])
        for ccx in range(CC):
            sl = xcol[pbuf][:, ccx * 512:(ccx + 1) * 512]
            dve("tensor_tensor", t2scr[:], sl, mBp[:], ALU.subtract,
                waits=[("ld", max(pair_ld[pp], rows_ld_d[pp]))], inc=False)
            dve("tensor_tensor", t2scr[:], t2scr[:], rstdBp[:], ALU.mult,
                inc=False)
            mB_used = dve("tensor_tensor", sl, sl, t2scr[:], ALU.add)

    load_rows(0)
    pair_ld[0] = prefetch_pair(0, 0, 0)
    pair_top(0)
    for g in range(8 * NP):
        p, j = g // 8, g % 8
        buf = p % 2

        for s in range(32):
            gen = s % 2
            prev_exp = (exp_cnt.get((j, s - 2), 0) if s >= 2
                        else exp_cnt.get(((j - 1) % 8, 30 + s), 0))
            mm(pS[gen][:, 0:512],
               kz[buf][0][:, s * 128: s * 128 + 128],
               qp[buf][:, j * 512: j * 512 + 512],
               True, True,
               waits=[("act", prev_exp), ("lqk", pair_qk_ld[p])])
            pe_sc = mm(pS[gen][:, 512:1024],
                       kz[buf][1][:, s * 128: s * 128 + 128],
                       qp[buf][:, j * 512: j * 512 + 512],
                       True, True, inc=True)
            if g >= 1 and s < 16:
                emit_av_chunks(g - 1, range(2 * s, 2 * s + 2))
            if s == 16 and g >= 1:
                emit_epilogue_a(g - 1)
            if s == 17 and g >= 1:
                emit_epilogue_b(g - 1, 0)
            if s == 18 and g >= 1:
                emit_epilogue_b(g - 1, 1)
            if s == 19 and j == 1 and p >= 1:
                finish_a(p - 1)
            if j == 1 and s > 19:
                drain_stats_mms(3)
            if s == 16 and j == 2:
                drain_stats_mms(16)
                if p >= 1:
                    finish_b_act(p - 1)
            if s == 20 and j == 2 and p >= 1:
                finish_b_mm1(p - 1)
            if s == 26 and j == 2 and p >= 1:
                finish_b_mm2(p - 1)
            if s == 20 and j == 4 and p >= 1:
                finish_b(p - 1, range(0, 4))
            if s == 26 and j == 4 and p >= 1:
                finish_b(p - 1, range(4, CC))
            wav = max(av_prog.get((0, (j - 1) % 8, s), 0),
                      av_prog.get((1, (j - 1) % 8, s), 0))
            exp_cnt[(j, s)] = act(
                P_all[:, 1024 * s: 1024 * (s + 1)], pS[gen][:], AF.Exp,
                waits=[("pe", max(wav, pe_sc))],
                scale=0.125)
        if j == 2 and p + 1 < NP:
            prefetch_qk(p + 1, pair_pe_done[p - 1] if p else 0)
        if j == 4 and p + 1 < NP:
            pair_ld[p + 1] = prefetch_rest(
                p + 1, xcol_free[p - 1] if p else 0)
        if j == 6 and p + 1 < NP:
            pair_top(p + 1)

    emit_av_chunks(8 * NP - 1, range(NT))
    emit_epilogue(8 * NP - 1)
    finish_a(NP - 1)
    drain_stats_mms(32)
    finish_b_act(NP - 1)
    finish_b_mm1(NP - 1)
    finish_b_mm2(NP - 1)
    finish_b(NP - 1)

    attn_pe = cnt["pe"]
    attn_dve = cnt["dve"]
    attn_act = cnt["act"]
    esB.close()

    g1 = strip1[:, 0:HCC * 1024]
    g2full = strip2
    g2 = g2full[:, 0:HCC * 1024]
    ln2b = g2full[:, 24 * 1024: 32 * 1024]

    esC = ExitStack()

    def sbC(name, shape, dtype):
        return esC.enter_context(nc.sbuf_tensor(name, shape, dtype))

    x2t = sbC("x2t", [128, CC * 1024], F32)
    wf = [sbC("wf%d" % i, [128, 2048], F32) for i in range(2)]
    wb = [sbC("wb0", [128, 4096], BF16), sbC("wb1", [128, 4096], BF16)]
    stmp = [sbC("stmp0", [128, 512], F32), sbC("stmp1", [128, 512], F32)]
    ysb = [sbC("ysb0", [128, 512], F32), sbC("ysb1", [128, 512], F32)]

    pW = [esC.enter_context(nc.psum_tensor("pW%d" % i, [128, 512], F32))
          for i in range(4)]

    pW_free = [("act", 0)] * 4
    wb_spill = [0, 0]
    wbf_done = (0, 0)
    wf_free = [0, 0]
    wfi = [0]
    wb_free = [0, 0]
    x2t_free = 0
    ysb_free = [0, 0]
    h2_pe_prev = 0

    def wf_load(src_ap, width=2048):
        s_ = wfi[0] % 2
        wfi[0] += 1
        dma(wf[s_][:, 0:width], src_ap, waits=[("dve", wf_free[s_])])
        return s_, cnt["ld"]

    for q4 in range(4):
        tok0 = 1024 * q4
        ls = ln2_store[2 * q4 + 1]
        gates = [("sln0", ls[0]), ("sln1", ls[1]), ("sln2", ls[2]),
                 ("sln3", ls[3]), ("act", attn_act if q4 == 0 else 0),
                 ("pe", attn_pe if q4 == 0 else h2_pe_prev)]
        dma(ln2b[:].rearrange("p (cc s) -> p cc s", cc=CC),
            ln2_s[:, tok0: tok0 + 1024].rearrange("(cc p) s -> p cc s", p=128),
            waits=gates)
        ln2b_ld = cnt["ld"]

        g1_done = 0
        x2_ld = 0
        for oc in range(HCC):
            i = oc % 2
            if oc == 2 or oc == 4:
                half = 0 if oc == 2 else 1
                dma(x2t[:, half * 4096:(half + 1) * 4096]
                    .rearrange("p (cc s) -> p cc s", cc=4),
                    x2_s[half * 512:(half + 1) * 512, tok0: tok0 + 1024]
                    .rearrange("(cc p) s -> p cc s", p=128),
                    waits=[("st", x2_store[2 * q4 + 1]), ("dve", x2t_free),
                           ("dve", attn_dve if q4 == 0 else 0)])
                x2_ld = cnt["ld"]
            if q4 == 0:
                s1_, l1 = wf_load(w1_t[oc], 1024)
                s2_, l2 = wf_load(w2_t[oc], 1024)
                dve("tensor_copy", wb[i][:, 0:1024], wf[s1_][:, 0:1024],
                    waits=[("ld", l1), ("pe", wb_free[i]),
                           ("swc%d" % i, wb_spill[i])], inc=False)
                wf_free[s1_] = cnt["dve"] + 1
                wf_free[s2_] = dve("tensor_copy", wb[i][:, 1024:2048],
                                   wf[s2_][:, 0:1024], waits=[("ld", l2)])
                wcast_d = cnt["dve"]
                dma(wbf_s[72 + oc, :, 0:2048], wb[i][:, 0:2048],
                    waits=[("dve", wcast_d)], kind="swc%d" % i, q="scalar")
                wb_spill[i] = cnt["swc%d" % i]
                wld = 0
            else:
                dma(wb[i][:, 0:2048], wbf_s[72 + oc, :, 0:2048],
                    waits=[("pe", wb_free[i]), ("swc0", wbf_done[0]),
                           ("swc1", wbf_done[1])])
                wld = cnt["ld"]
                wcast_d = 0
            for nt in range(2):
                sA_ = (2 * (oc * 2 + nt)) % 4
                sB_ = sA_ + 1
                for ccx in range(CC):
                    mm(pW[sA_][:], wb[i][:, ccx * 128: ccx * 128 + 128],
                       ln2b[:, ccx * 1024 + nt * 512: ccx * 1024 + nt * 512 + 512],
                       start=(ccx == 0), stop=(ccx == CC - 1),
                       waits=[("dve", wcast_d), ("ld", max(ln2b_ld, wld)),
                              pW_free[sA_]],
                       inc=(ccx == CC - 1))
                peA = cnt["pe"]
                for ccx in range(CC):
                    mm(pW[sB_][:], wb[i][:, 1024 + ccx * 128: 1024 + ccx * 128 + 128],
                       ln2b[:, ccx * 1024 + nt * 512: ccx * 1024 + nt * 512 + 512],
                       start=(ccx == 0), stop=(ccx == CC - 1),
                       waits=[pW_free[sB_]], inc=(ccx == CC - 1))
                peB = cnt["pe"]
                a_s = act(stmp[nt][:], pW[sA_][:], AF.Silu,
                          waits=[("pe", peA)], scale=s1_sb[:, oc:oc + 1])
                pW_free[sA_] = ("act", a_s)
                g1_done = dve("scalar_tensor_tensor",
                              g1[:, oc * 1024 + nt * 512: oc * 1024 + nt * 512 + 512],
                              pW[sB_][:], s2_sb[:, oc:oc + 1], stmp[nt][:],
                              ALU.mult, ALU.mult,
                              waits=[("pe", peB), ("act", a_s)])
                pW_free[sB_] = ("dve", g1_done)
            wb_free[i] = cnt["pe"]

        def h_stage(w_t, base, scol, src, dst, src_ready_dve, src_ready_act):
            done_act = 0
            for oc in range(HCC):
                i = oc % 2
                if q4 == 0:
                    for hfw in range(2):
                        s_, l_ = wf_load(w_t[oc, :, hfw * 2048:(hfw + 1) * 2048])
                        wf_free[s_] = dve(
                            "tensor_copy", wb[i][:, hfw * 2048:(hfw + 1) * 2048],
                            wf[s_][:],
                            waits=[("ld", l_), ("pe", wb_free[i]),
                                   ("swc%d" % i, wb_spill[i])])
                    wcast_d = cnt["dve"]
                    dma(wbf_s[base + oc], wb[i][:],
                        waits=[("dve", wcast_d)], kind="swc%d" % i, q="scalar")
                    wb_spill[i] = cnt["swc%d" % i]
                    wld = 0
                else:
                    dma(wb[i][:], wbf_s[base + oc],
                        waits=[("pe", wb_free[i]), ("swc0", wbf_done[0]),
                               ("swc1", wbf_done[1])])
                    wld = cnt["ld"]
                    wcast_d = 0
                for nt in range(2):
                    slot = (oc * 2 + nt) % 4
                    for ccx in range(HCC):
                        mm(pW[slot][:], wb[i][:, ccx * 128: ccx * 128 + 128],
                           src[:, ccx * 1024 + nt * 512: ccx * 1024 + nt * 512 + 512],
                           start=(ccx == 0), stop=(ccx == HCC - 1),
                           waits=[("dve", max(wcast_d, src_ready_dve)),
                                  ("act", src_ready_act), pW_free[slot],
                                  ("ld", wld)],
                           inc=(ccx == HCC - 1))
                    done_act = act(
                        dst[:, oc * 1024 + nt * 512: oc * 1024 + nt * 512 + 512],
                        pW[slot][:], AF.Gelu, waits=[("pe", cnt["pe"])],
                        scale=scol[:, oc:oc + 1])
                    pW_free[slot] = ("act", done_act)
                wb_free[i] = cnt["pe"]
            return done_act

        g2_done_act = h_stage(h1_t, 0, sh1_sb, g1, g2, g1_done, 0)
        h2_pe_start = cnt["pe"]
        g3_done_act = h_stage(h2_t, 32, sh2_sb, g2, g1, 0, g2_done_act)
        h2_pe_prev = cnt["pe"]

        for oc in range(CC):
            i = oc % 2
            if q4 == 0:
                for hfw in range(2):
                    s_, l_ = wf_load(w3_t[oc, :, hfw * 2048:(hfw + 1) * 2048])
                    wf_free[s_] = dve(
                        "tensor_copy", wb[i][:, hfw * 2048:(hfw + 1) * 2048],
                        wf[s_][:], waits=[("ld", l_), ("pe", wb_free[i]),
                                          ("swc%d" % i, wb_spill[i])])
                wcast_d = cnt["dve"]
                dma(wbf_s[64 + oc], wb[i][:], waits=[("dve", wcast_d)],
                    kind="swc%d" % i, q="scalar")
                wb_spill[i] = cnt["swc%d" % i]
                wld = 0
            else:
                dma(wb[i][:], wbf_s[64 + oc],
                    waits=[("pe", wb_free[i]), ("swc0", wbf_done[0]),
                           ("swc1", wbf_done[1])])
                wld = cnt["ld"]
                wcast_d = 0
            for nt in range(2):
                slot = (oc * 2 + nt) % 4
                for ccx in range(HCC):
                    mm(pW[slot][:], wb[i][:, ccx * 128: ccx * 128 + 128],
                       g1[:, ccx * 1024 + nt * 512: ccx * 1024 + nt * 512 + 512],
                       start=(ccx == 0), stop=(ccx == HCC - 1),
                       waits=[("dve", wcast_d), ("act", g3_done_act),
                              pW_free[slot], ("ld", wld)],
                       inc=(ccx == HCC - 1))
                syn = "sy%d" % nt
                d = dve("scalar_tensor_tensor", ysb[nt][:], pW[slot][:],
                        s3_sb[:, oc:oc + 1],
                        x2t[:, oc * 1024 + nt * 512: oc * 1024 + nt * 512 + 512],
                        ALU.mult, ALU.add,
                        waits=[("pe", cnt["pe"]), (syn, ysb_free[nt]),
                               ("ld", x2_ld)])
                pW_free[slot] = ("dve", d)
                x2t_free = d
                dma(yT[oc * 128:(oc + 1) * 128, tok0 + nt * 512: tok0 + nt * 512 + 512],
                    ysb[nt][:], waits=[("dve", d)], kind=syn, q="scalar")
                ysb_free[nt] = cnt[syn]
            wb_free[i] = cnt["pe"]
        if q4 == 0:
            wbf_done = (cnt["swc0"], cnt["swc1"])

    final_waits = [(nm, cnt[nm]) for nm in
                   ("st", "srm", "sq0", "sq1", "sv0", "sv1", "sln0",
                    "sln1", "sln2", "sln3", "swc0", "swc1", "sy0", "sy1")]
    emit("sync", final_waits, lambda e: e.nop())

    esC.close()

    with ExitStack() as semstack:
        for nm in ("ld", "st", "pe", "act", "dve", "srm", "sq0", "sq1",
                   "sv0", "sv1", "sln0", "sln1", "sln2", "sln3", "swc0",
                   "swc1", "sy0", "sy1", "lx0", "lx1", "lx2", "lx3",
                   "lw0", "lw1", "lqk"):
            sems[nm] = semstack.enter_context(nc.semaphore("s_" + nm))
        with nc.Block() as block:
            @block.sync
            def _(e):
                for f in ops["sync"]:
                    f(e)

            @block.tensor
            def _(e):
                for f in ops["tensor"]:
                    f(e)

            @block.scalar
            def _(e):
                for f in ops["scalar"]:
                    f(e)

            @block.vector
            def _(e):
                for f in ops["vector"]:
                    f(e)
    stack.close()
    return nc


def _tile_w(W, n_oc, n_cc):
    T = W.T.reshape(n_cc, 128, n_oc, 128)
    return np.ascontiguousarray(T.transpose(2, 1, 0, 3).reshape(n_oc, 128, n_cc * 128))


def kernel(L0_attn_queries, L0_attn_keys, L0_attn_values,
           L0_attn_Wq_w, L0_attn_Wq_b, L0_attn_Wk_w, L0_attn_Wk_b,
           L0_attn_Wv_w, L0_attn_Wv_b,
           L0_ffn_w1_w, L0_ffn_w1_s, L0_ffn_w2_w, L0_ffn_w2_s,
           L0_ffn_h1_w, L0_ffn_h1_s, L0_ffn_h2_w, L0_ffn_h2_s,
           L0_ffn_w3_w, L0_ffn_w3_s):
    f = lambda a: np.asarray(a, dtype=np.float32)
    x = f(L0_attn_queries)
    keys = f(L0_attn_keys)[0]
    values = f(L0_attn_values)[0]

    shared = {
        "keysT": np.ascontiguousarray(keys.T),
        "valuesT": np.ascontiguousarray(values.T),
        "wq_t": _tile_w(f(L0_attn_Wq_w), 8, 8),
        "wk_t": _tile_w(f(L0_attn_Wk_w), 8, 8),
        "wv_io": np.ascontiguousarray(f(L0_attn_Wv_w).T),
        "bq_t": np.ascontiguousarray(f(L0_attn_Wq_b).reshape(8, 128).T),
        "bk_t": np.ascontiguousarray(f(L0_attn_Wk_b).reshape(8, 128).T),
        "bv_row": np.ascontiguousarray(f(L0_attn_Wv_b).reshape(1, 1024)),
        "w1_t": _tile_w(f(L0_ffn_w1_w), 32, 8),
        "w2_t": _tile_w(f(L0_ffn_w2_w), 32, 8),
        "h1_t": _tile_w(f(L0_ffn_h1_w), 32, 32),
        "h2_t": _tile_w(f(L0_ffn_h2_w), 32, 32),
        "w3_t": _tile_w(f(L0_ffn_w3_w), 8, 32),
        "s1_t": np.ascontiguousarray(f(L0_ffn_w1_s).reshape(32, 128).T),
        "s2_t": np.ascontiguousarray(f(L0_ffn_w2_s).reshape(32, 128).T),
        "sh1_t": np.ascontiguousarray(f(L0_ffn_h1_s).reshape(32, 128).T),
        "sh2_t": np.ascontiguousarray(f(L0_ffn_h2_s).reshape(32, 128).T),
        "s3_t": np.ascontiguousarray(f(L0_ffn_w3_s).reshape(8, 128).T),
        "ident": np.eye(128, dtype=np.float32),
    }
    in_maps = []
    for b in range(8):
        m = dict(shared)
        m["xT"] = np.ascontiguousarray(x[b].T)
        in_maps.append(m)

    nc = build_program()
    res = run_bass_kernel_spmd(nc, in_maps, core_ids=list(range(8)))
    out = np.empty((8, 4096, 1024), dtype=np.float32)
    for b in range(8):
        out[b] = np.asarray(res.results[b]["yT"], dtype=np.float32).T
    return out

